# version 05
# baseline (speedup 1.0000x reference)
from contextlib import ExitStack

import numpy as np

import concourse.bass as bass
import concourse.mybir as mybir
from concourse.bass_types import AP
from concourse.bass import IndirectOffsetOnAxis
from concourse.bass_utils import run_bass_kernel_spmd

F32 = mybir.dt.float32
BF16 = mybir.dt.bfloat16
I32 = mybir.dt.int32
Alu = mybir.AluOpType
Act = mybir.ActivationFunctionType
AxX = mybir.AxisListType.X

D = 1024
NCH = 8
NH0, HD0 = 8, 128
NH1, HD1 = 16, 64
E0, MID0 = 32, 256
E1, NSH, MIDR, MIDS = 16, 2, 256, 512
MAXPOS = 512
RMS_EPS = 1e-6
LN_EPS = 1e-5
FORGET_LB = 0.9
BIG = 1.0e9


class Sem:
    def __init__(self, handle):
        self.h = handle
        self.n = 0

    def inc(self, k=1):
        self.n += k
        return self.n


class Seq:

    def __init__(self, b, name):
        self.se = b.sem(name + "_e")
        self.sh = b.sem(name + "_h")
        self.sg = b.sem(name + "_g")

    @property
    def val(self):
        return (self.se.n, self.sh.n, self.sg.n)

    def wait_all(self, e, upto):
        ve, vh, vg = upto
        if ve:
            e.wait_ge(self.se.h, ve)
        if vh:
            e.wait_ge(self.sh.h, vh)
        if vg:
            e.wait_ge(self.sg.h, vg)

    def drain(self, e, ins):
        n = self.se.inc(1)
        ins.then_inc(self.se.h, 1)
        e.wait_ge(self.se.h, n)

    def step(self, eng_dec, dma=False, gdma=False, n=1):
        before = self.val
        if gdma:
            sem, amt = self.sg, 16
        elif dma:
            sem, amt = self.sh, 16
        else:
            sem, amt = self.se, 1
            n = 1

        def deco(fn):
            @eng_dec
            def _(e, before=before, fn=fn, amt=amt, sem=sem, n=n):
                self.wait_all(e, before)
                ins = fn(e)
                sem.inc(amt * n)
                if isinstance(ins, list):
                    assert len(ins) == n, f"step declared n={n} but returned {len(ins)} DMAs"
                    for i2 in ins:
                        i2.then_inc(sem.h, amt)
                elif ins is not None:
                    assert n == 1, f"step declared n={n} but returned one instruction"
                    ins.then_inc(sem.h, amt)
                else:
                    assert amt == 1, "DMA step must return the dma instruction(s)"
                    e.sem_inc(sem.h, 1)
        return deco


class B:
    def __init__(self, nc, es, cfg):
        self.nc = nc
        self.es = es
        self.cfg = cfg
        self.sems = {}
        self._scopes = [es]

    def sem(self, name):
        if name not in self.sems:
            self.sems[name] = Sem(self.es.enter_context(self.nc.semaphore(name)))
        return self.sems[name]

    def sb(self, name, shape, dtype):
        return self._scopes[-1].enter_context(self.nc.sbuf_tensor(name, list(shape), dtype))

    def ps(self, name, shape, dtype):
        return self._scopes[-1].enter_context(self.nc.psum_tensor(name, list(shape), dtype))

    def push(self):
        self._scopes.append(ExitStack())

    def pop(self):
        self._scopes.pop().close()


def emit_ln_stats(seq, v, z, sq, st, eps):
    seq.drain(v, v.tensor_tensor(sq[:], z[:], z[:], Alu.mult))
    v.reduce_sum(st[:, 0:1], z[:], axis=AxX)
    seq.drain(v, v.reduce_sum(st[:, 1:2], sq[:], axis=AxX))
    v.tensor_scalar(st[:, 0:1], st[:, 0:1], 1.0 / D, None, Alu.mult)
    seq.drain(v, v.tensor_scalar(st[:, 1:2], st[:, 1:2], 1.0 / D, None, Alu.mult))
    seq.drain(v, v.tensor_tensor(st[:, 2:3], st[:, 0:1], st[:, 0:1], Alu.mult))
    seq.drain(v, v.tensor_tensor(st[:, 1:2], st[:, 1:2], st[:, 2:3], Alu.subtract))
    seq.drain(v, v.tensor_scalar(st[:, 1:2], st[:, 1:2], 1.0, eps, Alu.mult, Alu.add))
    return v.reciprocal(st[:, 2:3], st[:, 1:2])


def emit_ln_apply(main, block, zt, st, out):
    @main.step(block.scalar)
    def _(sc):
        return sc.activation(out=st[:, 3:4], in_=st[:, 2:3], func=Act.Sqrt)

    @main.step(block.vector)
    def _(v):
        main.drain(v, v.tensor_tensor(st[:, 1:2], st[:, 0:1], st[:, 3:4], Alu.mult))
        return v.tensor_scalar(st[:, 1:2], st[:, 1:2], -1.0, None, Alu.mult)

    @main.step(block.scalar)
    def _(sc):
        return sc.activation(out=out, in_=zt, func=Act.Identity,
                             scale=st[:, 3:4], bias=st[:, 1:2])


def build_program(cfg):
    T = cfg["T"]
    stage = cfg["stage"]
    NTT = T // 128
    NT512 = T // 512

    nc = bass.Bass("TRN2", target_bir_lowering=False, detect_race_conditions=False)

    dx = nc.dram_tensor("L0_attn_x", [T, D], F32, kind="ExternalInput")
    dnormw = nc.dram_tensor("L0_attn_norm_w", [D], F32, kind="ExternalInput")
    dwf = nc.dram_tensor("L0_attn_w_forget", [D, D], F32, kind="ExternalInput")
    dwi = nc.dram_tensor("L0_attn_w_input", [D, D], F32, kind="ExternalInput")
    dwo = nc.dram_tensor("L0_attn_w_output", [D, D], F32, kind="ExternalInput")
    dqoff = nc.dram_tensor("qoff", [2], I32, kind="ExternalInput")
    daux = nc.dram_tensor("aux_off", [64], I32, kind="ExternalInput")
    dgix = nc.dram_tensor("gidx", [128, 64], I32, kind="ExternalInput")

    dWq = nc.dram_tensor("L1_attn_Wq", [D, D], F32, kind="ExternalInput")
    dbq = nc.dram_tensor("L1_attn_bq", [D], F32, kind="ExternalInput")
    dWk = nc.dram_tensor("L1_attn_Wk", [D, D], F32, kind="ExternalInput")
    dbk = nc.dram_tensor("L1_attn_bk", [D], F32, kind="ExternalInput")
    dWv = nc.dram_tensor("L1_attn_Wv", [D, D], F32, kind="ExternalInput")
    dbv = nc.dram_tensor("L1_attn_bv", [D], F32, kind="ExternalInput")
    dWo = nc.dram_tensor("L1_attn_Wo", [D, D], F32, kind="ExternalInput")
    dbo = nc.dram_tensor("L1_attn_bo", [D], F32, kind="ExternalInput")
    dhg = nc.dram_tensor("L1_attn_head_gates", [NH1], F32, kind="ExternalInput")
    drpb = nc.dram_tensor("L1_attn_rel_pos_bias", [2 * MAXPOS - 1, NH1], F32,
                          kind="ExternalInput")

    drw1 = nc.dram_tensor("L1_ffn_router_w", [D, E1], F32, kind="ExternalInput")
    dwgr = nc.dram_tensor("L1_ffn_wg_r", [E1, D, MIDR], F32, kind="ExternalInput")
    dwur = nc.dram_tensor("L1_ffn_wu_r", [E1, D, MIDR], F32, kind="ExternalInput")
    dwdr = nc.dram_tensor("L1_ffn_wd_r", [E1, MIDR, D], F32, kind="ExternalInput")
    dwgs = nc.dram_tensor("L1_ffn_wg_s", [NSH, D, MIDS], F32, kind="ExternalInput")
    dwus = nc.dram_tensor("L1_ffn_wu_s", [NSH, D, MIDS], F32, kind="ExternalInput")
    dwds = nc.dram_tensor("L1_ffn_wd_s", [NSH, MIDS, D], F32, kind="ExternalInput")

    drw0 = nc.dram_tensor("L0_ffn_router_w", [E0, D], F32, kind="ExternalInput")
    drb0 = nc.dram_tensor("L0_ffn_router_b", [E0], F32, kind="ExternalInput")
    dw1 = nc.dram_tensor("L0_ffn_weights1", [E0, D, MID0], F32, kind="ExternalInput")
    db1 = nc.dram_tensor("L0_ffn_bias1", [E0, MID0], F32, kind="ExternalInput")
    dw2 = nc.dram_tensor("L0_ffn_weights2", [E0, MID0, D], F32, kind="ExternalInput")
    db2 = nc.dram_tensor("L0_ffn_bias2", [E0, D], F32, kind="ExternalInput")

    TOWN = cfg["TOWN"]
    if stage.startswith("d_"):
        out_rows = 128
    else:
        out_rows = T if stage in ("x1", "x2") else TOWN
    if stage.startswith("d_"):
        dy = nc.dram_tensor("y", [128, NCH * T], F32, kind="ExternalOutput")
    else:
        dy = nc.dram_tensor("y", [out_rows, D], F32, kind="ExternalOutput")

    def emit_dump(b, block, main, src_ap):
        @main.step(block.gpsimd, gdma=True)
        def _(g):
            return g.dma_start(out=dy.ap(), in_=src_ap)

        @main.step(block.sync)
        def _(s):
            return None

    dres1 = nc.dram_tensor("sc_res1", [T, D], F32, kind="Internal")
    dres2 = nc.dram_tensor("sc_res2", [T, D], F32, kind="Internal")
    dyffn = nc.dram_tensor("sc_yffn", [D, T], F32, kind="Internal")
    dvsum = nc.dram_tensor("sc_vsum", [NH0, T], F32, kind="Internal")
    dcomb = nc.dram_tensor("sc_comb", [E1, T], F32, kind="Internal")
    dzrow = nc.dram_tensor("sc_zrow", [512], F32, kind="Internal")
    dres3 = nc.dram_tensor("sc_res3", [TOWN, D], F32, kind="Internal")
    dx2fm = nc.dram_tensor("sc_x2fm", [128, NCH, T], BF16, kind="Internal")
    EXT_W = 8576
    ESH = 4224
    dext = nc.dram_tensor("sc_ext", [NH1, EXT_W], BF16, kind="Internal")
    PADW = EXT_W - 127
    dext_pad = nc.dram_tensor("sc_extpad", [NH1, 128, PADW], BF16, kind="Internal")
    dtblbf = nc.dram_tensor("sc_tblbf", [2 * MAXPOS - 1, NH1], BF16, kind="Internal")

    es = ExitStack()
    with es:
        b = B(nc, es, cfg)
        idn = b.sb("idn", [128, 128], BF16)

        xn_fm = b.sb("xn_fm", [128, NCH, T], BF16)
        y_fm = b.sb("y_fm", [128, NCH, T], BF16)
        normw_bc = b.sb("normw_bc", [128, D], F32)

        with nc.Block() as block:
            main = Seq(b, "main")

            iot = b.sb("iot", [128, 128], F32)
            nwrow = b.sb("nwrow", [1, D], F32)

            @main.step(block.sync, dma=True)
            def _(s):
                return s.dma_start(out=nwrow[:], in_=dnormw.ap()[None, :])

            colsel_i = b.sb("colsel_i", [128, NH0, NH0], F32)
            colsel = b.sb("colsel", [128, NH0, NH0], BF16)

            @main.step(block.gpsimd)
            def _(g):
                g.iota(iot[:], pattern=[[1, 128]], base=0, channel_multiplier=-1,
                       allow_small_or_imprecise_dtypes=True)
                return g.iota(colsel_i[:], pattern=[[-1, NH0], [1, NH0]], base=0,
                              channel_multiplier=0, allow_small_or_imprecise_dtypes=True)

            @main.step(block.gpsimd, gdma=True)
            def _(g):
                return g.dma_start(out=normw_bc[:], in_=AP(dnormw, 0, [[0, 128], [1, D]]))

            @main.step(block.vector)
            def _(v):
                v.tensor_single_scalar(idn[:], iot[:], 0.0, Alu.is_equal)
                return v.tensor_single_scalar(colsel[:], colsel_i[:], 0.0, Alu.is_equal)

            if stage == "d_nw":
                @main.step(block.gpsimd, gdma=True)
                def _(g):
                    return g.dma_start(out=AP(dy.tensor if hasattr(dy, "tensor") else dy, 0,
                                              [[NCH * T, 128], [1, D]]), in_=normw_bc[:])

                @main.step(block.sync)
                def _(s):
                    return None

            if stage == "d_idn":
                @main.step(block.gpsimd, gdma=True)
                def _(g):
                    return g.dma_start(out=AP(dy.tensor if hasattr(dy, "tensor") else dy, 0,
                                              [[NCH * T, 128], [1, 128]]), in_=idn[:])

                @main.step(block.sync)
                def _(s):
                    return None

            ps_tr = b.ps("ps_tr", [128, NCH, 128], BF16)
            b.push()
            wiv_sb = b.sb("wiv_sb", [NH0, D], F32)
            wivT = b.sb("wivT", [128, NCH, NH0], BF16)
            vsum_sb = b.sb("vsum_sb", [NH0, T], F32)
            xt = b.sb("xt", [128, D], F32)
            xnt = b.sb("xnt", [128, D], BF16)
            sq = b.sb("sq", [128, D], F32)
            st = b.sb("st", [128, 4], F32)

            def tile_to_fm(src_bf_sb, dst_fm, i):
                @main.step(block.tensor)
                def _(t):
                    for c in range(NCH):
                        ins = t.transpose(ps_tr[:, c, :], src_bf_sb[:, c * 128:(c + 1) * 128], idn[:])
                    return ins

                @main.step(block.vector)
                def _(v):
                    return v.tensor_copy(dst_fm[:, :, i * 128:(i + 1) * 128], ps_tr[:])

            for i in range(NTT):
                @main.step(block.sync, dma=True)
                def _(s, i=i):
                    return s.dma_start(out=xt[:], in_=dx.ap()[i * 128:(i + 1) * 128, :])

                @main.step(block.vector)
                def _(v):
                    main.drain(v, v.tensor_tensor(sq[:], xt[:], xt[:], Alu.mult))
                    main.drain(v, v.reduce_sum(st[:, 0:1], sq[:], axis=AxX))
                    main.drain(v, v.tensor_scalar(st[:, 1:2], st[:, 0:1], 1.0 / D, RMS_EPS,
                                                  Alu.mult, Alu.add))
                    return v.reciprocal(st[:, 2:3], st[:, 1:2])

                @main.step(block.scalar)
                def _(sc):
                    return sc.activation(out=st[:, 3:4], in_=st[:, 2:3], func=Act.Sqrt)

                @main.step(block.scalar)
                def _(sc):
                    return sc.activation(out=sq[:], in_=xt[:], func=Act.Identity,
                                         scale=st[:, 3:4])

                @main.step(block.vector)
                def _(v):
                    return v.tensor_tensor(xnt[:], sq[:], normw_bc[:], Alu.mult)

                if stage == "d_xnt":
                    @main.step(block.gpsimd, gdma=True)
                    def _(g, i=i):
                        return g.dma_start(out=AP(dy.tensor if hasattr(dy, "tensor") else dy,
                                                  i * 128 * NCH * T // T if False else i * 128 * 1024,
                                                  [[1024, 128], [1, 1024]]), in_=xnt[:])

                tile_to_fm(xnt, xn_fm, i)

            if stage == "d_xn":
                emit_dump(b, block, main, xn_fm[:].rearrange("p c t -> p (c t)"))

            b.push()
            wchunk = b.sb("wchunk", [128, D], BF16)
            wiv_bf = b.sb("wiv_bf", [NH0, D], BF16)
            ps_wt = b.ps("ps_wt", [128, NCH, NH0], BF16)
            ps_vs8 = b.ps("ps_vs8", [NH0, 2, 512], F32)
            ps_vs = b.ps("ps_vs", [NH0, 512], F32)

            for c in range(NCH):
                @main.step(block.gpsimd, gdma=True)
                def _(g, c=c):
                    return g.dma_start(out=wchunk[:], in_=dwi.ap()[c * 128:(c + 1) * 128, :])

                @main.step(block.tensor)
                def _(t, c=c):
                    for j in range(2):
                        ins = t.matmul(ps_vs8[:, j, :], colsel[:, c, :],
                                       wchunk[:, j * 512:(j + 1) * 512],
                                       start=(c == 0), stop=(c == NCH - 1))
                    return ins

            @main.step(block.vector)
            def _(v):
                return v.tensor_copy(wiv_sb[:], ps_vs8[:].rearrange("h j f -> h (j f)"))

            @main.step(block.vector)
            def _(v):
                return v.tensor_copy(wiv_bf[:], wiv_sb[:])

            @main.step(block.tensor)
            def _(t):
                for c in range(NCH):
                    ins = t.transpose(ps_wt[:, c, :], wiv_bf[:, c * 128:(c + 1) * 128],
                                      idn[0:NH0, 0:NH0])
                return ins

            @main.step(block.vector)
            def _(v):
                return v.tensor_copy(wivT[:], ps_wt[:])

            for j in range(NT512):
                @main.step(block.tensor)
                def _(t, j=j):
                    for c in range(NCH):
                        ins = t.matmul(ps_vs[:], wivT[:, c, :],
                                       xn_fm[:, c, j * 512:(j + 1) * 512],
                                       start=(c == 0), stop=(c == NCH - 1))
                    return ins

                @main.step(block.vector)
                def _(v, j=j):
                    return v.tensor_copy(vsum_sb[:, j * 512:(j + 1) * 512], ps_vs[:])

            @main.step(block.sync, dma=True)
            def _(s):
                return s.dma_start(out=dvsum.ap(), in_=vsum_sb[:])

            b.pop()

            b.push()
            THL = min(T, 1024)
            NHV = T // THL
            NJH = THL // 512
            f_sb = b.sb("f_sb", [128, THL], F32)
            u_sb = b.sb("u_sb", [128, THL], F32)
            s_pp = [b.sb(f"s_pp{i}", [128, THL], F32) for i in range(2)]
            vbc = b.sb("vbc", [128, THL], F32)
            w_nat = b.sb("w_nat", [128, D], BF16)
            wT = b.sb("wT", [128, NCH, 128], BF16)
            ps_w = b.ps("ps_w", [128, NCH, 128], BF16)
            ps_g = b.ps("ps_g", [128, 512], F32)
            tmp512 = b.sb("tmp512", [128, 512], F32)

            def load_wT(dw, c):
                @main.step(block.gpsimd, gdma=True)
                def _(g, c=c):
                    return g.dma_start(out=w_nat[:], in_=dw.ap()[c * 128:(c + 1) * 128, :])

                @main.step(block.tensor)
                def _(t):
                    for ic in range(NCH):
                        ins = t.transpose(ps_w[:, ic, :], w_nat[:, ic * 128:(ic + 1) * 128], idn[:])
                    return ins

                @main.step(block.vector)
                def _(v):
                    return v.tensor_copy(wT[:], ps_w[:])

            for c in range(NCH):
                for hh in range(NHV):
                    h0 = hh * THL
                    s_cur, s_prev = s_pp[hh % 2], s_pp[(hh + 1) % 2]

                    @main.step(block.sync, dma=True)
                    def _(s, c=c, h0=h0):
                        return s.dma_start(out=vbc[:],
                                           in_=AP(dvsum, c * T + h0, [[0, 128], [1, THL]]))

                    load_wT(dwf, c)
                    for j in range(NJH):
                        @main.step(block.tensor)
                        def _(t, j=j, h0=h0):
                            for ic in range(NCH):
                                ins = t.matmul(ps_g[:], wT[:, ic, :],
                                               xn_fm[:, ic, h0 + j * 512:h0 + (j + 1) * 512],
                                               start=(ic == 0), stop=(ic == NCH - 1))
                            return ins

                        @main.step(block.scalar)
                        def _(sc):
                            return sc.activation(out=tmp512[:], in_=ps_g[:], func=Act.Sigmoid)

                        @main.step(block.vector)
                        def _(v, j=j):
                            sl = slice(j * 512, (j + 1) * 512)
                            main.drain(v, v.tensor_scalar_max(f_sb[:, sl], tmp512[:], FORGET_LB))
                            main.drain(v, v.tensor_scalar(u_sb[:, sl], f_sb[:, sl], -1.0, 1.0,
                                                          Alu.mult, Alu.add))
                            return v.tensor_tensor(u_sb[:, sl], u_sb[:, sl], vbc[:, sl], Alu.mult)

                    @main.step(block.vector)
                    def _(v, hh=hh, s_cur=s_cur, s_prev=s_prev):
                        if hh > 0:
                            main.drain(v, v.tensor_tensor(st[:, 0:1], f_sb[:, 0:1],
                                                          s_prev[:, THL - 1:THL], Alu.mult))
                            main.drain(v, v.tensor_tensor(u_sb[:, 0:1], u_sb[:, 0:1],
                                                          st[:, 0:1], Alu.add))
                        return v.tensor_tensor_scan(s_cur[:], f_sb[:], u_sb[:], 0.0,
                                                    Alu.mult, Alu.add)

                    load_wT(dwo, c)
                    for j in range(NJH):
                        @main.step(block.tensor)
                        def _(t, j=j, h0=h0):
                            for ic in range(NCH):
                                ins = t.matmul(ps_g[:], wT[:, ic, :],
                                               xn_fm[:, ic, h0 + j * 512:h0 + (j + 1) * 512],
                                               start=(ic == 0), stop=(ic == NCH - 1))
                            return ins

                        @main.step(block.scalar)
                        def _(sc):
                            return sc.activation(out=tmp512[:], in_=ps_g[:], func=Act.Sigmoid)

                        @main.step(block.vector)
                        def _(v, c=c, j=j, h0=h0, s_cur=s_cur):
                            sl = slice(j * 512, (j + 1) * 512)
                            return v.tensor_tensor(y_fm[:, c, h0 + j * 512:h0 + (j + 1) * 512],
                                                   tmp512[:], s_cur[:, sl], Alu.mult)

            b.pop()

            if stage == "d_y":
                emit_dump(b, block, main, y_fm[:].rearrange("p c t -> p (c t)"))

            b.push()
            x1_fm = xn_fm
            zt = b.sb("zt", [128, D], F32)
            x1t = b.sb("x1t", [128, D], F32)
            x1bf = b.sb("x1bf", [128, D], BF16)
            ps_yt = b.ps("ps_yt", [128, NCH, 128], BF16)

            for i in range(NTT):
                @main.step(block.sync, dma=True)
                def _(s, i=i):
                    return s.dma_start(out=xt[:], in_=dx.ap()[i * 128:(i + 1) * 128, :])

                @main.step(block.tensor)
                def _(t, i=i):
                    for c in range(NCH):
                        ins = t.transpose(ps_yt[:, c, :], y_fm[:, c, i * 128:(i + 1) * 128], idn[:])
                    return ins

                @main.step(block.vector)
                def _(v):
                    main.drain(v, v.scalar_tensor_tensor(zt[:], xt[:], 2.0,
                                                         ps_yt[:].rearrange("p c f -> p (c f)"),
                                                         Alu.mult, Alu.add))
                    return emit_ln_stats(main, v, zt, sq, st, LN_EPS)

                emit_ln_apply(main, block, zt[:], st, x1t[:])

                @main.step(block.vector)
                def _(v):
                    return v.tensor_copy(x1bf[:], x1t[:])

                @main.step(block.sync, dma=True)
                def _(s, i=i):
                    return s.dma_start(out=dres1.ap()[i * 128:(i + 1) * 128, :], in_=x1t[:])

                tile_to_fm(x1bf, x1_fm, i)

            b.pop()
            b.pop()

            if stage in ("x2", "x3", "final"):
                b.push()
                x1_fm = xn_fm
                x2_fm = y_fm

                ones_row = b.sb("ones_row", [1, 512], BF16)
                m_fm = b.sb("m_fm", [E0, T], BF16)
                b.push()
                idn32 = b.sb("idn32", [128, 128], F32)
                rw_nat = b.sb("rw_nat", [E0, D], BF16)
                rwT = b.sb("rwT", [128, NCH, E0], BF16)
                rb_col = b.sb("rb_col", [E0, 1], F32)
                lg_fm = b.sb("lg_fm", [E0, T], F32)
                ps_r32 = b.ps("ps_r32", [128, E0], F32)
                ps_lg = b.ps("ps_lg", [E0, 512], F32)
                ps_m32 = b.ps("ps_m32", [E0, 128], BF16)
                lgt = b.sb("lgt", [128, E0], F32)
                eqt = b.sb("eqt", [128, E0], F32)
                cumt = b.sb("cumt", [128, E0], F32)
                zeros32 = b.sb("zeros32", [128, E0], F32)
                onht = b.sb("onht", [128, E0], BF16)
                mx1 = b.sb("mx1", [128, 1], F32)

                @main.step(block.vector)
                def _(v):
                    v.memset(ones_row[:], 1.0)
                    v.memset(zeros32[:], 0.0)
                    return v.tensor_single_scalar(idn32[:], iot[:], 0.0, Alu.is_equal)

                @main.step(block.gpsimd, gdma=True)
                def _(g):
                    return g.dma_start(out=rw_nat[:], in_=drw0.ap())

                @main.step(block.sync, dma=True)
                def _(s):
                    return s.dma_start(out=rb_col[:], in_=drb0.ap()[:, None])

                ps_rwt = b.ps("ps_rwt", [128, E0], BF16)
                for ic in range(NCH):
                    @main.step(block.tensor)
                    def _(t, ic=ic):
                        return t.transpose(ps_rwt[:], rw_nat[:, ic * 128:(ic + 1) * 128],
                                           idn[0:E0, 0:E0])

                    @main.step(block.vector)
                    def _(v, ic=ic):
                        return v.tensor_copy(rwT[:, ic, :], ps_rwt[:])

                for j in range(NT512):
                    @main.step(block.tensor)
                    def _(t, j=j):
                        for ic in range(NCH):
                            ins = t.matmul(ps_lg[:], rwT[:, ic, :],
                                           x1_fm[:, ic, j * 512:(j + 1) * 512],
                                           start=(ic == 0), stop=(ic == NCH - 1))
                        return ins

                    @main.step(block.scalar)
                    def _(sc, j=j):
                        return sc.activation(out=lg_fm[:, j * 512:(j + 1) * 512], in_=ps_lg[:],
                                             func=Act.Identity, bias=rb_col[:])

                for i in range(NTT):
                    @main.step(block.tensor)
                    def _(t, i=i):
                        return t.transpose(ps_r32[:, 0:E0], lg_fm[:, i * 128:(i + 1) * 128],
                                           idn32[0:E0, 0:E0])

                    @main.step(block.vector)
                    def _(v, i=i):
                        main.drain(v, v.tensor_copy(lgt[:], ps_r32[:, 0:E0]))
                        main.drain(v, v.reduce_max(mx1[:], lgt[:], axis=AxX))
                        return v.tensor_scalar(mx1[:], mx1[:], -1.0, None, Alu.mult)

                    @main.step(block.scalar)
                    def _(sc):
                        return sc.activation(out=eqt[:], in_=lgt[:], func=Act.Identity,
                                             bias=mx1[:])

                    @main.step(block.vector)
                    def _(v, i=i):
                        main.drain(v, v.tensor_single_scalar(eqt[:], eqt[:], 0.0, Alu.is_equal))
                        main.drain(v, v.tensor_tensor_scan(cumt[:], eqt[:], zeros32[:], 0.0,
                                                           Alu.add, Alu.add))
                        main.drain(v, v.tensor_single_scalar(cumt[:], cumt[:], 1.0, Alu.is_le))
                        return v.tensor_tensor(onht[:], eqt[:], cumt[:], Alu.mult)

                    @main.step(block.tensor)
                    def _(t, i=i):
                        return t.transpose(ps_m32[:], onht[:], idn[:])

                    @main.step(block.vector)
                    def _(v, i=i):
                        return v.tensor_copy(m_fm[:, i * 128:(i + 1) * 128], ps_m32[:])

                b.pop()

                b.push()
                S2H = 8 if T >= 2048 else 1
                THALF = T // S2H
                NJ = THALF // 512
                NG = 8
                GE = 4
                m2fm = b.sb("m2fm", [E0, T], BF16)
                m2p0 = b.sb("m2p0", [1, E0, THALF], BF16)
                b1p0 = b.sb("b1p0", [1, E0, MID0], BF16)
                b2all = b.sb("b2all", [E0, D], BF16)
                W1SZ = NCH * GE * MID0
                W2SZ = 2 * GE * D
                if NCH * T >= W1SZ + W2SZ + 2 * GE * 512:
                    w1g = AP(y_fm, 0, [[NCH * T, 128], [GE * MID0, NCH], [MID0, GE], [1, MID0]])
                    w2g = AP(y_fm, W1SZ, [[NCH * T, 128], [GE * D, 2], [D, GE], [1, D]])
                    h_all = AP(y_fm, W1SZ + W2SZ, [[NCH * T, 128], [GE * 512, 2], [512, GE], [1, 512]])
                else:
                    w1g = b.sb("w1g", [128, NCH, GE, MID0], BF16)[:]
                    w2g = b.sb("w2g", [128, 2, GE, D], BF16)[:]
                    h_all = b.sb("h_all", [128, 2, GE, 512], BF16)[:]
                if NCH * T >= W1SZ + W2SZ + 2 * GE * 512 + 4096:
                    ysb = AP(y_fm, W1SZ + W2SZ + 2 * GE * 512,
                             [[NCH * T, 128], [1, 4096]]).bitcast(F32)
                else:
                    ysb = b.sb("ysb_t", [128, 2048], F32)[:]
                ps_h = b.ps("ps_h", [128, 2, 512], F32)
                ps_y4 = b.ps("ps_y4", [128, 4, 512], F32)

                @main.step(block.vector)
                def _(v):
                    return v.tensor_scalar(m2fm[:], m_fm[:], BIG, -BIG, Alu.mult, Alu.add)

                @main.step(block.gpsimd, gdma=True, n=2)
                def _(g):
                    i1 = g.dma_start(out=b1p0[:], in_=db1.ap()[None, :, :])
                    return [i1, g.dma_start(out=b2all[:], in_=db2.ap())]

                for half in range(S2H):
                    ht0 = half * THALF

                    @main.step(block.sync, dma=True, n=E0)
                    def _(s, ht0=ht0):
                        return [s.dma_start(out=m2p0[:, e, :], in_=m2fm[e:e + 1, ht0:ht0 + THALF])
                                for e in range(E0)]

                    for g_i in range(NG):
                        @main.step(block.gpsimd, gdma=True, n=2 * GE)
                        def _(g, g_i=g_i):
                            out = []
                            for ei in range(GE):
                                e = g_i * GE + ei
                                out.append(g.dma_start(out=w1g[:, :, ei, :],
                                           in_=dw1.ap()[e].rearrange("(c p) m -> p c m", p=128)))
                                out.append(g.dma_start(out=w2g[:, :, ei, :],
                                           in_=dw2.ap()[e].rearrange("(c p) m -> p c m", p=128)))
                            return out

                        for j in range(NJ):
                            t0 = ht0 + j * 512
                            for ei in range(GE):
                                e = g_i * GE + ei

                                @main.step(block.tensor)
                                def _(t, ei=ei, e=e, t0=t0, j=j):
                                    for mc in range(2):
                                        for ic in range(NCH):
                                            t.matmul(ps_h[:, mc, :], w1g[:, ic, ei, mc * 128:(mc + 1) * 128],
                                                     x1_fm[:, ic, t0:t0 + 512],
                                                     start=(ic == 0), stop=False)
                                        t.matmul(ps_h[:, mc, :], b1p0[:, e, mc * 128:(mc + 1) * 128],
                                                 ones_row[:, 0:512], start=False, stop=False)
                                        ins = t.matmul(ps_h[:, mc, :], ones_row[:, 0:128],
                                                       m2p0[:, e, j * 512:(j + 1) * 512],
                                                       start=False, stop=True)
                                    return ins

                                @main.step(block.scalar)
                                def _(sc, ei=ei):
                                    return sc.activation(out=h_all[:, :, ei, :], in_=ps_h[:],
                                                         func=Act.Relu)

                            for och in range(2):
                                @main.step(block.tensor)
                                def _(t, och=och, g_i=g_i, t0=t0):
                                    for oc4 in range(4):
                                        oc = och * 4 + oc4
                                        for ei in range(GE):
                                            for mc in range(2):
                                                last = (ei == GE - 1 and mc == 1 and g_i != 0)
                                                ins = t.matmul(ps_y4[:, oc4, :],
                                                               w2g[:, mc, ei, oc * 128:(oc + 1) * 128],
                                                               h_all[:, mc, ei, :],
                                                               start=(ei == 0 and mc == 0), stop=last)
                                        if g_i == 0:
                                            ins = t.matmul(ps_y4[:, oc4, :],
                                                           b2all[:, oc * 128:(oc + 1) * 128],
                                                           m_fm[:, t0:t0 + 512], start=False, stop=True)
                                    return ins

                                @main.step(block.vector)
                                def _(v):
                                    return v.tensor_copy(ysb, ps_y4[:].rearrange("p a b -> p (a b)"))

                                @main.step(block.gpsimd, gdma=True)
                                def _(g, g_i=g_i, t0=t0, och=och):
                                    dst = AP(dyffn, och * 4 * 128 * T + t0,
                                             [[T, 128], [128 * T, 4], [1, 512]])
                                    srcy = ysb.rearrange("p (a b) -> p a b", a=4)
                                    if g_i == 0:
                                        return g.dma_start(out=dst, in_=srcy)
                                    return g.dma_start(out=dst, in_=srcy, accum_op=Alu.add)

                b.pop()

                x1t = b.sb("x1t_2", [128, D], F32)
                x2t = b.sb("x2t", [128, D], F32)
                x2bf = b.sb("x2bf", [128, D], BF16)
                yf32 = b.sb("yf32", [128, NCH, 128], F32)
                ybf = b.sb("ybf", [128, NCH, 128], BF16)
                sq2 = b.sb("sq2", [128, D], F32)
                st2 = b.sb("st2", [128, 4], F32)
                zt2 = b.sb("zt2", [128, D], F32)
                ps_yt2 = b.ps("ps_yt2", [128, NCH, 128], BF16)

                for i in range(NTT):
                    @main.step(block.sync, dma=True, n=2)
                    def _(s, i=i):
                        i1 = s.dma_start(out=x1t[:], in_=dres1.ap()[i * 128:(i + 1) * 128, :])
                        src = AP(dyffn, i * 128, [[T, 128], [128 * T, NCH], [1, 128]])
                        return [i1, s.dma_start(out=yf32[:], in_=src)]

                    @main.step(block.scalar)
                    def _(sc):
                        return sc.activation(out=ybf[:], in_=yf32[:], func=Act.Relu)

                    @main.step(block.tensor)
                    def _(t):
                        for c in range(NCH):
                            ins = t.transpose(ps_yt2[:, c, :], ybf[:, c, :], idn[:])
                        return ins

                    @main.step(block.vector)
                    def _(v):
                        main.drain(v, v.tensor_tensor(zt2[:], x1t[:],
                                                      ps_yt2[:].rearrange("p c f -> p (c f)"),
                                                      Alu.add))
                        return emit_ln_stats(main, v, zt2, sq2, st2, LN_EPS)

                    emit_ln_apply(main, block, zt2[:], st2, x2t[:])

                    @main.step(block.vector)
                    def _(v):
                        return v.tensor_copy(x2bf[:], x2t[:])

                    @main.step(block.sync, dma=True)
                    def _(s, i=i):
                        return s.dma_start(out=dres2.ap()[i * 128:(i + 1) * 128, :], in_=x2t[:])

                    @main.step(block.tensor)
                    def _(t, i=i):
                        for c in range(NCH):
                            ins = t.transpose(ps_yt2[:, c, :], x2bf[:, c * 128:(c + 1) * 128], idn[:])
                        return ins

                    @main.step(block.vector)
                    def _(v, i=i):
                        return v.tensor_copy(x2_fm[:, :, i * 128:(i + 1) * 128], ps_yt2[:])

                    @main.step(block.sync, dma=True)
                    def _(s, i=i):
                        dst = AP(dx2fm, i * 128, [[NCH * T, 128], [T, NCH], [1, 128]])
                        return s.dma_start(out=dst, in_=x2_fm[:, :, i * 128:(i + 1) * 128])

                b.pop()

            if stage in ("x3", "final"):
                b.push()
                x2_fm = y_fm
                NRT = TOWN // 128
                NGQ = TOWN // 512
                NST = T // 128
                NS512 = T // 512
                WLO = 128 * (NRT - 1)
                W1S = T + WLO
                W2S = T + TOWN - 128
                TBL = 2 * MAXPOS - 1


                avn_fm = xn_fm[:, :, 0:TOWN]
                x2own_fm = xn_fm[:, :, TOWN:2 * TOWN]
                aofm = y_fm[:, :, 0:TOWN]

                gix = b.sb("gix", [128, 64], I32)
                ones1 = b.sb("ones1", [1, 512], BF16)
                kaug = b.sb("kaug", [65, T], BF16)

                vaug = b.sb("vaug", [128, NST, 65], BF16)
                qaug = b.sb("qaug", [65, TOWN], BF16)
                strip1 = b.sb("strip1", [128, W1S], BF16)
                strip2 = b.sb("strip2", [128, W2S], BF16)
                stm = b.sb("stm", [128, T], F32)
                if T >= 4096:
                    vfm = stm[0:64, 0:T // 2].bitcast(BF16)
                    bqr = stm[0:1, 2048:2560].bitcast(BF16)
                    bkr = stm[0:1, 2560:3072].bitcast(BF16)
                    bvr = stm[0:1, 3072:3584].bitcast(BF16)
                    bor = stm[0:1, 3584:4096].bitcast(BF16)
                    wnat3 = stm[:, 0:512].bitcast(BF16)
                    wT3r = stm[:, 512:1024].bitcast(BF16)
                else:
                    vfm = b.sb("vfm", [64, T], BF16)[:]
                    bqr = b.sb("bqr", [1, D], BF16)[:]
                    bkr = b.sb("bkr", [1, D], BF16)[:]
                    bvr = b.sb("bvr", [1, D], BF16)[:]
                    bor = b.sb("bor", [1, D], BF16)[:]
                    wnat3 = b.sb("wnat3", [128, D], BF16)[:]
                    wT3r = b.sb("wT3r", [128, D], BF16)[:]
                candA = b.sb("candA", [128, 256], F32)
                candB = b.sb("candB", [128, 256], F32)
                v64 = b.sb("v64", [128, 64], F32)
                taubf = b.sb("taubf", [128, 1], BF16)
                trow_sb = b.sb("trow_sb", [1, 128], BF16)
                p_sb = b.sb("p_sb", [128, 512], BF16)
                pm_sb = b.sb("pm_sb", [128, 512], BF16)
                if T >= 4096:
                    zsb = stm[:, 0:512]
                    z0 = stm[0:1, 512:1024]
                    z0i = stm[0:1, 1024:1536]
                    rbc = stm[0:64, 1536:2048]
                    avtmp = stm[0:64, 2048:2304].bitcast(BF16)
                else:
                    zsb = b.sb("zsb", [128, 512], F32)[:]
                    z0 = b.sb("z0", [1, 512], F32)[:]
                    z0i = b.sb("z0i", [1, 512], F32)[:]
                    rbc = b.sb("rbc", [64, 512], F32)[:]
                    avtmp = b.sb("avtmp", [64, 512], BF16)[:]
                w64 = b.sb("w64", [64, D], BF16)
                wTh = b.sb("wTh", [128, NCH, 64], BF16)
                hg_sb = b.sb("hg_sb", [1, NH1], F32)
                ps_kv = b.ps("ps_kv", [64, 512], F32)
                ps_s = b.ps("ps_s", [128, 512], F32)
                ps_ts = b.ps("ps_ts", [128, 512], F32)
                ps_av = b.ps("ps_av", [65, 512], F32)
                ps_tb = b.ps("ps_tb", [128, 128], BF16)
                ps_trow = b.ps("ps_trow", [1, 128], BF16)

                @main.step(block.sync, dma=True, n=2)
                def _(s):
                    return [s.dma_start(out=gix[:], in_=dgix.ap()),
                            s.dma_start(out=hg_sb[:], in_=dhg.ap()[None, :])]

                @main.step(block.vector)
                def _(v):
                    v.memset(ones1[:], 1.0)
                    v.memset(kaug[64:65, :], 1.0)
                    return v.memset(vaug[:, :, 64:65], 1.0)

                @main.step(block.gpsimd, gdma=True)
                def _(g):
                    return g.dma_start(out=AP(dtblbf, 0, [[NH1, TBL], [1, NH1]]),
                                       in_=AP(drpb, 0, [[NH1, TBL], [1, NH1]]))

                @main.step(block.gpsimd, gdma=True, n=4 * NH1)
                def _(g):
                    out = []
                    for h in range(NH1):
                        base = h * EXT_W
                        out.append(g.dma_start(out=AP(dext, base, [[1, 2690], [1, 1]]),
                                   in_=AP(dtblbf, h, [[0, 2690], [1, 1]])))
                        out.append(g.dma_start(out=AP(dext, base + 2690, [[1, TBL], [1, 1]]),
                                   in_=AP(dtblbf, h, [[NH1, TBL], [1, 1]]),
                                   allow_slow_non_contiguous=True))
                        out.append(g.dma_start(out=AP(dext, base + 3713, [[1, TBL], [1, 1]]),
                                   in_=AP(dtblbf, h, [[NH1, TBL], [1, 1]]),
                                   allow_slow_non_contiguous=True))
                        out.append(g.dma_start(out=AP(dext, base + 4736, [[1, EXT_W - 4736], [1, 1]]),
                                   in_=AP(dtblbf, 1022 * NH1 + h, [[0, EXT_W - 4736], [1, 1]])))
                    return out

                @main.step(block.gpsimd, gdma=True, n=NCH)
                def _(g):
                    sp = AP(dx2fm, 0, [[1, 128 * NCH * T], [1, 1]])
                    return [g.indirect_dma_start(x2own_fm[:, c, :], None, sp,
                                                 IndirectOffsetOnAxis(gix[:, 40 + c:41 + c], 0))
                            for c in range(NCH)]

                @main.step(block.gpsimd, gdma=True)
                def _(g):
                    return g.dma_start(out=dext_pad.ap(),
                                       in_=AP(dext, 127, [[EXT_W, NH1], [-1, 128], [1, PADW]]))

                def load_wTh(dW, h):
                    @main.step(block.gpsimd, gdma=True)
                    def _(g, h=h):
                        return g.dma_start(out=w64[:], in_=dW.ap()[h * 64:(h + 1) * 64, :])

                    for ic in range(NCH):
                        @main.step(block.tensor)
                        def _(t, ic=ic):
                            return t.transpose(ps_tb[:, 0:64], w64[:, ic * 128:(ic + 1) * 128],
                                               idn[0:64, 0:64])

                        @main.step(block.vector)
                        def _(v, ic=ic):
                            return v.tensor_copy(wTh[:, ic, :], ps_tb[:, 0:64])

                def load_wT3(dW, c):
                    @main.step(block.gpsimd, gdma=True)
                    def _(g, c=c):
                        return g.dma_start(out=wnat3, in_=dW.ap()[c * 128:(c + 1) * 128, :])

                    @main.step(block.tensor)
                    def _(t):
                        for ic in range(NCH):
                            ins = t.transpose(ps_tr[:, ic, :], wnat3[:, ic * 128:(ic + 1) * 128], idn[:])
                        return ins

                    @main.step(block.vector)
                    def _(v):
                        return v.tensor_copy(wT3r, ps_tr[:].rearrange("p c f -> p (c f)"))

                for h in range(NH1):
                    ch, rr = h // 2, (h % 2) * 64

                    @main.step(block.gpsimd, gdma=True, n=3)
                    def _(g):
                        return [g.dma_start(out=bqr, in_=dbq.ap()[None, :]),
                                g.dma_start(out=bkr, in_=dbk.ap()[None, :]),
                                g.dma_start(out=bvr, in_=dbv.ap()[None, :])]

                    load_wTh(dWk, h)
                    for j in range(NS512):
                        @main.step(block.tensor)
                        def _(t, j=j, h=h):
                            for ic in range(NCH):
                                t.matmul(ps_kv[:], wTh[:, ic, :], x2_fm[:, ic, j * 512:(j + 1) * 512],
                                         start=(ic == 0), stop=False)
                            return t.matmul(ps_kv[:], bkr[:, h * 64:(h + 1) * 64],
                                            ones1[:, 0:512], start=False, stop=True)

                        @main.step(block.vector)
                        def _(v, j=j):
                            return v.tensor_copy(kaug[0:64, j * 512:(j + 1) * 512], ps_kv[:])

                    load_wTh(dWv, h)
                    for j in range(NS512):
                        @main.step(block.tensor)
                        def _(t, j=j, h=h):
                            for ic in range(NCH):
                                t.matmul(ps_kv[:], wTh[:, ic, :], x2_fm[:, ic, j * 512:(j + 1) * 512],
                                         start=(ic == 0), stop=False)
                            return t.matmul(ps_kv[:], bvr[:, h * 64:(h + 1) * 64],
                                            ones1[:, 0:512], start=False, stop=True)

                        @main.step(block.vector)
                        def _(v, j=j):
                            return v.tensor_copy(vfm[:, j * 512:(j + 1) * 512], ps_kv[:])

                    for st in range(NST):
                        @main.step(block.tensor)
                        def _(t, st=st):
                            return t.transpose(ps_tb[:, 0:64], vfm[:, st * 128:(st + 1) * 128],
                                               idn[0:64, 0:64])

                        @main.step(block.vector)
                        def _(v, st=st):
                            return v.tensor_copy(vaug[:, st, 0:64], ps_tb[:, 0:64])

                    load_wTh(dWq, h)
                    for j in range(TOWN // 512):
                        @main.step(block.tensor)
                        def _(t, j=j, h=h):
                            for ic in range(NCH):
                                t.matmul(ps_kv[:], wTh[:, ic, :], x2own_fm[:, ic, j * 512:(j + 1) * 512],
                                         start=(ic == 0), stop=False)
                            return t.matmul(ps_kv[:], bqr[:, h * 64:(h + 1) * 64],
                                            ones1[:, 0:512], start=False, stop=True)

                        @main.step(block.scalar)
                        def _(sc, j=j):
                            return sc.activation(out=qaug[0:64, j * 512:(j + 1) * 512],
                                                 in_=ps_kv[:], func=Act.Copy, scale=0.125)

                    @main.step(block.gpsimd, gdma=True, n=2)
                    def _(g, h=h):
                        sp = AP(dext_pad, 0, [[1, NH1 * 128 * PADW], [1, 1]])
                        return [g.indirect_dma_start(strip1[:], None, sp,
                                IndirectOffsetOnAxis(gix[:, h:h + 1], 0)),
                                g.indirect_dma_start(strip2[:], None, sp,
                                IndirectOffsetOnAxis(gix[:, 16 + h:17 + h], 0))]

                    for rt in range(NRT):
                        for j in range(NS512):
                            @main.step(block.tensor)
                            def _(t, rt=rt, j=j):
                                t.matmul(ps_s[:], qaug[0:64, rt * 128:(rt + 1) * 128],
                                         kaug[0:64, j * 512:(j + 1) * 512], start=True, stop=False)
                                w0 = j * 512 - 128 * rt + WLO
                                return t.matmul(ps_s[:], idn[:], strip1[:, w0:w0 + 512],
                                                start=False, stop=True)

                            @main.step(block.scalar)
                            def _(sc, j=j):
                                return sc.copy(stm[:, j * 512:(j + 1) * 512], ps_s[:])

                        CSZ = T // 32
                        @main.step(block.vector)
                        def _(v, CSZ=CSZ):
                            for cc in range(32):
                                ins = v.max(out=candA[:, cc * 8:(cc + 1) * 8],
                                            in_=stm[:, cc * CSZ:(cc + 1) * CSZ])
                            return ins

                        ncand = 256
                        @main.step(block.vector)
                        def _(v, ncand=ncand):
                            src, dst = candA, candB
                            for r in range(8):
                                main.drain(v, v.max(out=v64[:, r * 8:(r + 1) * 8],
                                                    in_=src[:, 0:ncand]))
                                if r < 7:
                                    main.drain(v, v.match_replace(
                                        out=dst[:, 0:ncand],
                                        in_to_replace=v64[:, r * 8:(r + 1) * 8],
                                        in_values=src[:, 0:ncand], imm_value=-3e38))
                                src, dst = dst, src
                            return v.tensor_scalar(taubf[:], v64[:, 63:64], -1.0, None, Alu.mult)

                        @main.step(block.tensor)
                        def _(t):
                            return t.transpose(ps_trow[:], taubf[:], idn[:])

                        @main.step(block.vector)
                        def _(v):
                            return v.tensor_copy(trow_sb[:], ps_trow[:])

                        @main.step(block.sync, dma=True)
                        def _(s, rt=rt):
                            return s.dma_start(out=qaug[64:65, rt * 128:(rt + 1) * 128],
                                               in_=trow_sb[:])

                    for g_i in range(NGQ):
                        for st in range(NST):
                            @main.step(block.tensor)
                            def _(t, g_i=g_i, st=st):
                                t.matmul(ps_ts[:], kaug[:, st * 128:(st + 1) * 128],
                                         qaug[:, g_i * 512:(g_i + 1) * 512], start=True, stop=False)
                                w0 = g_i * 512 - st * 128 + (T - 128)
                                return t.matmul(ps_ts[:], idn[:], strip2[:, w0:w0 + 512],
                                                start=False, stop=True)

                            @main.step(block.scalar)
                            def _(sc):
                                return sc.activation(out=p_sb[:], in_=ps_ts[:], func=Act.Exp)

                            @main.step(block.vector)
                            def _(v):
                                return v.scalar_tensor_tensor(pm_sb[:], p_sb[:], 1.0, p_sb[:],
                                                              Alu.is_ge, Alu.mult)

                            @main.step(block.tensor)
                            def _(t, st=st):
                                return t.matmul(ps_av[:], vaug[:, st, :], pm_sb[:],
                                                start=(st == 0), stop=(st == NST - 1))

                        @main.step(block.vector)
                        def _(v):
                            return v.tensor_copy(zsb[64:65], ps_av[64:65, :])

                        @main.step(block.sync, dma=True)
                        def _(s):
                            return s.dma_start(out=z0, in_=zsb[64:65])

                        @main.step(block.vector)
                        def _(v):
                            return v.reciprocal(z0i, z0)

                        @main.step(block.scalar)
                        def _(sc, h=h):
                            return sc.activation(out=z0i, in_=z0i, func=Act.Identity,
                                                 scale=hg_sb[0:1, h:h + 1])

                        @main.step(block.sync, dma=True)
                        def _(s):
                            return s.dma_start(out=dzrow.ap()[None, :], in_=z0i)

                        @main.step(block.sync, dma=True)
                        def _(s):
                            return s.dma_start(out=rbc, in_=AP(dzrow, 0, [[0, 64], [1, 512]]))

                        @main.step(block.vector)
                        def _(v):
                            return v.tensor_tensor(avtmp, ps_av[0:64, :], rbc, Alu.mult)

                        @main.step(block.gpsimd, gdma=True)
                        def _(g, ch=ch, rr=rr, g_i=g_i):
                            dst = avn_fm[rr:rr + 64, ch, g_i * 512:(g_i + 1) * 512]
                            return g.dma_start(out=dst, in_=avtmp)

                @main.step(block.gpsimd, gdma=True)
                def _(g):
                    return g.dma_start(out=bor, in_=dbo.ap()[None, :])

                for oc in range(NCH):
                    load_wT3(dWo, oc)
                    for j in range(TOWN // 512):
                        @main.step(block.tensor)
                        def _(t, oc=oc, j=j):
                            for ic in range(NCH):
                                t.matmul(ps_s[:], wT3r[:, ic * 128:(ic + 1) * 128],
                                         avn_fm[:, ic, j * 512:(j + 1) * 512],
                                         start=(ic == 0), stop=False)
                            return t.matmul(ps_s[:], bor[:, oc * 128:(oc + 1) * 128],
                                            ones1[:, 0:512], start=False, stop=True)

                        @main.step(block.vector)
                        def _(v, oc=oc, j=j):
                            return v.tensor_copy(aofm[:, oc, j * 512:(j + 1) * 512], ps_s[:])

                x3_fm = x2own_fm
                if T >= 4096:
                    x2t3 = stm[:, 0:1024]
                    zt3 = stm[:, 1024:2048]
                    x3t = stm[:, 2048:3072]
                    sq3 = stm[:, 3072:4096]
                else:
                    x2t3 = b.sb("x2t3", [128, D], F32)[:]
                    zt3 = b.sb("zt3", [128, D], F32)[:]
                    x3t = b.sb("x3t", [128, D], F32)[:]
                    sq3 = b.sb("sq3", [128, D], F32)[:]
                x3bf = b.sb("x3bf", [128, D], BF16)
                st3 = v64[:, 0:4]

                for i in range(NRT):
                    @main.step(block.gpsimd, gdma=True)
                    def _(g, i=i):
                        sp = AP(dres2, 0, [[1, T * D], [1, 1]])
                        return g.indirect_dma_start(x2t3, None, sp,
                                                    IndirectOffsetOnAxis(gix[:, 24 + i:25 + i], 0))

                    @main.step(block.tensor)
                    def _(t, i=i):
                        for c in range(NCH):
                            ins = t.transpose(ps_tr[:, c, :], aofm[:, c, i * 128:(i + 1) * 128], idn[:])
                        return ins

                    @main.step(block.vector)
                    def _(v):
                        main.drain(v, v.tensor_tensor(zt3, x2t3,
                                                      ps_tr[:].rearrange("p c f -> p (c f)"),
                                                      Alu.add))
                        return emit_ln_stats(main, v, zt3, sq3, st3, LN_EPS)

                    emit_ln_apply(main, block, zt3, st3, x3t)

                    @main.step(block.vector)
                    def _(v):
                        return v.tensor_copy(x3bf[:], x3t)

                    @main.step(block.sync, dma=True)
                    def _(s, i=i):
                        return s.dma_start(out=dres3.ap()[i * 128:(i + 1) * 128, :], in_=x3t)

                    @main.step(block.tensor)
                    def _(t, i=i):
                        for c in range(NCH):
                            ins = t.transpose(ps_tr[:, c, :], x3bf[:, c * 128:(c + 1) * 128], idn[:])
                        return ins

                    @main.step(block.vector)
                    def _(v, i=i):
                        return v.tensor_copy(x3_fm[:, :, i * 128:(i + 1) * 128], ps_tr[:])

                b.pop()

            if stage == "final":
                b.push()
                x3_fm = xn_fm[:, :, TOWN:2 * TOWN]
                out_fm = xn_fm[:, :, 0:TOWN]
                NTO = TOWN // 512
                NRT = TOWN // 128

                rw1 = b.sb("rw1", [128, NCH, E1], BF16)
                lg1 = b.sb("lg1", [E1, TOWN], F32)
                comb_fm = b.sb("comb_fm", [E1, TOWN], F32)
                idn32b = b.sb("idn32b", [128, 128], F32)
                lgt1 = b.sb("lgt1", [128, E1], F32)
                prb = b.sb("prb", [128, E1], F32)
                eq1 = b.sb("eq1", [128, E1], F32)
                cu1 = b.sb("cu1", [128, E1], F32)
                oh1 = b.sb("oh1", [128, E1], F32)
                pm1 = b.sb("pm1", [128, E1], F32)
                eq2 = b.sb("eq2", [128, E1], F32)
                oh2 = b.sb("oh2", [128, E1], F32)
                zeros16 = b.sb("zeros16", [128, E1], F32)
                combt = b.sb("combt", [128, E1], F32)
                stt4 = b.sb("stt4", [128, 8], F32)
                combbc = b.sb("combbc", [128, 512], F32)
                wgs_sb = b.sb("wgs_sb", [128, NCH, MIDS], BF16)
                wus_sb = b.sb("wus_sb", [128, NCH, MIDS], BF16)
                if NCH * T >= 20480 + 4096 + 2048:
                    wds_sb = AP(y_fm, 20480, [[NCH * T, 128], [D, 4], [1, D]])
                    wdr_sb = AP(y_fm, 24576, [[NCH * T, 128], [D, 2], [1, D]])
                else:
                    wds_sb = b.sb("wds_sb", [128, 4, D], BF16)[:]
                    wdr_sb = b.sb("wdr_sb", [128, 2, D], BF16)[:]
                wgr_sb = b.sb("wgr_sb", [128, NCH, MIDR], BF16)
                wur_sb = b.sb("wur_sb", [128, NCH, MIDR], BF16)
                sgt = b.sb("sgt", [128, 512], BF16)
                sgc = b.sb("sgc", [128, 512], BF16)
                ps_g1 = b.ps("ps_g1", [128, 512], F32)
                ps_g2 = b.ps("ps_g2", [128, 512], F32)
                ps_o4 = b.ps("ps_o4", [128, 4, 512], F32)
                ps_rt4 = b.ps("ps_rt4", [128, 512], F32)
                ps_lg1 = ps_rt4[0:E1, :]
                ps_t16 = ps_rt4[:, 0:128]

                if NCH * T >= (4 * NSH + 2 * E1) * 512:
                    def hsh_ap(mc, e):
                        return AP(y_fm, (mc * NSH + e) * 512, [[NCH * T, 128], [1, 512]])

                    def hrt_ap(mc, e):
                        return AP(y_fm, 4 * NSH * 512 + (mc * E1 + e) * 512,
                                  [[NCH * T, 128], [1, 512]])
                else:
                    hsh_t = b.sb("hsh_t", [128, 4 * NSH, 512], BF16)
                    hrt_t = b.sb("hrt_t", [128, 2 * E1, 512], BF16)

                    def hsh_ap(mc, e):
                        return hsh_t[:, mc * NSH + e, :]

                    def hrt_ap(mc, e):
                        return hrt_t[:, mc * E1 + e, :]

                @main.step(block.vector)
                def _(v):
                    v.memset(zeros16[:], 0.0)
                    return v.tensor_single_scalar(idn32b[:], iot[:], 0.0, Alu.is_equal)

                @main.step(block.gpsimd, gdma=True)
                def _(g):
                    return g.dma_start(out=rw1[:], in_=drw1.ap().rearrange("(c p) e -> p c e", p=128))

                for j in range(NTO):
                    @main.step(block.tensor)
                    def _(t, j=j):
                        for ic in range(NCH):
                            ins = t.matmul(ps_lg1[:], rw1[:, ic, :],
                                           x3_fm[:, ic, j * 512:(j + 1) * 512],
                                           start=(ic == 0), stop=(ic == NCH - 1))
                        return ins

                    @main.step(block.vector)
                    def _(v, j=j):
                        return v.tensor_copy(lg1[:, j * 512:(j + 1) * 512], ps_lg1[:])

                for i in range(NRT):
                    @main.step(block.tensor)
                    def _(t, i=i):
                        return t.transpose(ps_t16[:, 0:E1],
                                           lg1[:, i * 128:(i + 1) * 128], idn32b[0:E1, 0:E1])

                    @main.step(block.vector)
                    def _(v):
                        main.drain(v, v.tensor_copy(lgt1[:], ps_t16[:, 0:E1]))
                        return v.reduce_max(stt4[:, 0:1], lgt1[:], axis=AxX)

                    @main.step(block.scalar)
                    def _(sc):
                        return sc.activation(out=prb[:], in_=lgt1[:], func=Act.Exp,
                                             bias=stt4[:, 0:1], scale=-1.0)

                    @main.step(block.vector)
                    def _(v):
                        main.drain(v, v.reciprocal(prb[:], prb[:]))
                        main.drain(v, v.reduce_sum(stt4[:, 1:2], prb[:], axis=AxX))
                        return v.reciprocal(stt4[:, 2:3], stt4[:, 1:2])

                    @main.step(block.scalar)
                    def _(sc):
                        return sc.activation(out=prb[:], in_=prb[:], func=Act.Identity,
                                             scale=stt4[:, 2:3])

                    @main.step(block.vector)
                    def _(v):
                        main.drain(v, v.reduce_max(stt4[:, 3:4], prb[:], axis=AxX))
                        return v.tensor_scalar(stt4[:, 7:8], stt4[:, 3:4], -1.0, None, Alu.mult)

                    @main.step(block.scalar)
                    def _(sc):
                        return sc.activation(out=eq1[:], in_=prb[:], func=Act.Identity,
                                             bias=stt4[:, 7:8])

                    @main.step(block.vector)
                    def _(v):
                        main.drain(v, v.tensor_single_scalar(eq1[:], eq1[:], 0.0, Alu.is_equal))
                        main.drain(v, v.tensor_tensor_scan(cu1[:], eq1[:], zeros16[:], 0.0,
                                                           Alu.add, Alu.add))
                        main.drain(v, v.tensor_single_scalar(cu1[:], cu1[:], 1.0, Alu.is_le))
                        main.drain(v, v.tensor_tensor(oh1[:], eq1[:], cu1[:], Alu.mult))
                        main.drain(v, v.tensor_tensor(pm1[:], prb[:], oh1[:], Alu.mult))
                        main.drain(v, v.tensor_tensor(pm1[:], prb[:], pm1[:], Alu.subtract))
                        main.drain(v, v.reduce_max(stt4[:, 4:5], pm1[:], axis=AxX))
                        return v.tensor_scalar(stt4[:, 7:8], stt4[:, 4:5], -1.0, None, Alu.mult)

                    @main.step(block.scalar)
                    def _(sc):
                        return sc.activation(out=eq2[:], in_=pm1[:], func=Act.Identity,
                                             bias=stt4[:, 7:8])

                    @main.step(block.vector)
                    def _(v):
                        main.drain(v, v.tensor_single_scalar(eq2[:], eq2[:], 0.0, Alu.is_equal))
                        main.drain(v, v.tensor_tensor_scan(cu1[:], eq2[:], zeros16[:], 0.0,
                                                           Alu.add, Alu.add))
                        main.drain(v, v.tensor_single_scalar(cu1[:], cu1[:], 1.0, Alu.is_le))
                        main.drain(v, v.tensor_tensor(oh2[:], eq2[:], cu1[:], Alu.mult))
                        main.drain(v, v.tensor_tensor(stt4[:, 5:6], stt4[:, 3:4], stt4[:, 4:5],
                                                      Alu.add))
                        main.drain(v, v.tensor_scalar(stt4[:, 5:6], stt4[:, 5:6], 1.0, 1e-9,
                                                      Alu.mult, Alu.add))
                        main.drain(v, v.reciprocal(stt4[:, 6:7], stt4[:, 5:6]))
                        main.drain(v, v.tensor_tensor(stt4[:, 3:4], stt4[:, 3:4], stt4[:, 6:7],
                                                      Alu.mult))
                        return v.tensor_tensor(stt4[:, 4:5], stt4[:, 4:5], stt4[:, 6:7],
                                               Alu.mult)

                    @main.step(block.scalar)
                    def _(sc):
                        sc.activation(out=oh1[:], in_=oh1[:], func=Act.Identity,
                                      scale=stt4[:, 3:4])
                        return sc.activation(out=oh2[:], in_=oh2[:], func=Act.Identity,
                                             scale=stt4[:, 4:5])

                    @main.step(block.vector)
                    def _(v):
                        return v.tensor_tensor(combt[:], oh1[:], oh2[:], Alu.add)

                    @main.step(block.tensor)
                    def _(t):
                        return t.transpose(ps_t16[0:E1, 0:128], combt[:], idn32b[:])

                    @main.step(block.vector)
                    def _(v, i=i):
                        return v.tensor_copy(comb_fm[:, i * 128:(i + 1) * 128],
                                             ps_t16[0:E1, 0:128])

                @main.step(block.sync, dma=True)
                def _(s):
                    return s.dma_start(out=dcomb.ap()[:, 0:TOWN], in_=comb_fm[:])

                for j in range(NTO):
                    t0 = j * 512


                    for e in range(NSH):
                        @main.step(block.gpsimd, gdma=True, n=2)
                        def _(g, e=e):
                            return [g.dma_start(out=wgs_sb[:], in_=dwgs.ap()[e].rearrange("(c p) m -> p c m", p=128)),
                                    g.dma_start(out=wus_sb[:], in_=dwus.ap()[e].rearrange("(c p) m -> p c m", p=128))]

                        for mc in range(4):
                            @main.step(block.tensor)
                            def _(t, e=e, mc=mc, t0=t0):
                                for ic in range(NCH):
                                    t.matmul(ps_g1[:], wgs_sb[:, ic, mc * 128:(mc + 1) * 128],
                                             x3_fm[:, ic, t0:t0 + 512], start=(ic == 0),
                                             stop=(ic == NCH - 1))
                                for ic in range(NCH):
                                    ins = t.matmul(ps_g2[:], wus_sb[:, ic, mc * 128:(mc + 1) * 128],
                                                   x3_fm[:, ic, t0:t0 + 512], start=(ic == 0),
                                                   stop=(ic == NCH - 1))
                                return ins

                            @main.step(block.scalar)
                            def _(sc):
                                return sc.activation(out=sgt[:], in_=ps_g1[:], func=Act.Silu)

                            @main.step(block.vector)
                            def _(v, e=e, mc=mc):
                                return v.tensor_tensor(hsh_ap(mc, e), ps_g2[:], sgt[:], Alu.mult)

                    for e in range(E1):
                        @main.step(block.gpsimd, gdma=True, n=2)
                        def _(g, e=e):
                            return [g.dma_start(out=wgr_sb[:], in_=dwgr.ap()[e].rearrange("(c p) m -> p c m", p=128)),
                                    g.dma_start(out=wur_sb[:], in_=dwur.ap()[e].rearrange("(c p) m -> p c m", p=128))]

                        @main.step(block.sync, dma=True)
                        def _(s, e=e, t0=t0):
                            return s.dma_start(out=combbc[:],
                                               in_=AP(dcomb, e * T + t0, [[0, 128], [1, 512]]))

                        for mc in range(2):
                            @main.step(block.tensor)
                            def _(t, e=e, mc=mc, t0=t0):
                                for ic in range(NCH):
                                    t.matmul(ps_g1[:], wgr_sb[:, ic, mc * 128:(mc + 1) * 128],
                                             x3_fm[:, ic, t0:t0 + 512], start=(ic == 0),
                                             stop=(ic == NCH - 1))
                                for ic in range(NCH):
                                    ins = t.matmul(ps_g2[:], wur_sb[:, ic, mc * 128:(mc + 1) * 128],
                                                   x3_fm[:, ic, t0:t0 + 512], start=(ic == 0),
                                                   stop=(ic == NCH - 1))
                                return ins

                            @main.step(block.scalar)
                            def _(sc):
                                return sc.activation(out=sgt[:], in_=ps_g1[:], func=Act.Silu)

                            @main.step(block.vector)
                            def _(v, e=e, mc=mc):
                                main.drain(v, v.tensor_tensor(sgc[:], sgt[:], combbc[:],
                                                              Alu.mult))
                                return v.tensor_tensor(hrt_ap(mc, e), ps_g2[:], sgc[:], Alu.mult)

                    for och in range(2):
                        for e in range(NSH):
                            @main.step(block.gpsimd, gdma=True)
                            def _(g, e=e):
                                return g.dma_start(out=wds_sb,
                                                   in_=dwds.ap()[e].rearrange("(c p) m -> p c m", p=128))

                            @main.step(block.tensor)
                            def _(t, e=e, och=och):
                                for oc4 in range(4):
                                    oc = och * 4 + oc4
                                    for mc in range(4):
                                        ins = t.matmul(ps_o4[:, oc4, :],
                                                       wds_sb[:, mc, oc * 128:(oc + 1) * 128],
                                                       hsh_ap(mc, e),
                                                       start=(e == 0 and mc == 0), stop=False)
                                return ins

                        for e in range(E1):
                            @main.step(block.gpsimd, gdma=True)
                            def _(g, e=e):
                                return g.dma_start(out=wdr_sb,
                                                   in_=dwdr.ap()[e].rearrange("(c p) m -> p c m", p=128))

                            @main.step(block.tensor)
                            def _(t, e=e, och=och):
                                for oc4 in range(4):
                                    oc = och * 4 + oc4
                                    for mc in range(2):
                                        last = (e == E1 - 1 and mc == 1)
                                        ins = t.matmul(ps_o4[:, oc4, :],
                                                       wdr_sb[:, mc, oc * 128:(oc + 1) * 128],
                                                       hrt_ap(mc, e),
                                                       start=False, stop=last)
                                return ins

                        @main.step(block.vector)
                        def _(v, och=och, t0=t0):
                            for oc4 in range(4):
                                oc = och * 4 + oc4
                                ins = v.tensor_copy(out_fm[:, oc, t0:t0 + 512], ps_o4[:, oc4, :])
                            return ins

                x3t4 = b.sb("x3t4", [128, D], F32)
                x4t = b.sb("x4t", [128, D], F32)
                sq4 = b.sb("sq4", [128, D], F32)
                st4 = b.sb("st4", [128, 4], F32)
                zt4 = b.sb("zt4", [128, D], F32)

                for i in range(NRT):
                    @main.step(block.sync, dma=True)
                    def _(s, i=i):
                        return s.dma_start(out=x3t4[:], in_=dres3.ap()[i * 128:(i + 1) * 128, :])

                    @main.step(block.tensor)
                    def _(t, i=i):
                        for c in range(NCH):
                            ins = t.transpose(ps_tr[:, c, :], out_fm[:, c, i * 128:(i + 1) * 128], idn[:])
                        return ins

                    @main.step(block.vector)
                    def _(v):
                        main.drain(v, v.tensor_tensor(zt4[:], x3t4[:],
                                                      ps_tr[:].rearrange("p c f -> p (c f)"),
                                                      Alu.add))
                        return emit_ln_stats(main, v, zt4, sq4, st4, LN_EPS)

                    emit_ln_apply(main, block, zt4[:], st4, x4t[:])

                    @main.step(block.sync, dma=True)
                    def _(s, i=i):
                        return s.dma_start(out=dy.ap()[i * 128:(i + 1) * 128, :], in_=x4t[:])

                @main.step(block.sync)
                def _(s):
                    return None

                b.pop()

            if stage in ("x1", "x2", "x3"):
                dsrc = {"x1": dres1, "x2": dres2, "x3": dres3}[stage]
                xo = b.sb("xo", [128, D], F32)
                for i in range(out_rows // 128):
                    @main.step(block.sync, dma=True)
                    def _(s, i=i):
                        return s.dma_start(out=xo[:], in_=dsrc.ap()[i * 128:(i + 1) * 128, :])

                    @main.step(block.sync, dma=True)
                    def _(s, i=i):
                        return s.dma_start(out=dy.ap()[i * 128:(i + 1) * 128, :], in_=xo[:])

                @main.step(block.sync)
                def _(s):
                    return None

    return nc


_PROG_CACHE = {}


def _get_prog(key, cfg):
    if key not in _PROG_CACHE:
        _PROG_CACHE[key] = build_program(cfg)
    return _PROG_CACHE[key]


def declared_inputs(nc):
    names = set()
    for a in nc.m.functions[0].allocations:
        mls = getattr(a, "memorylocations", None)
        if getattr(a, "kind", None) == "ExternalInput" and mls:
            for ml in mls:
                names.add(ml.name)
    return names


def make_in_maps(inputs, nc, ncores=8, T=4096):
    want = declared_inputs(nc)
    x = np.asarray(inputs["L0_attn_x"])
    shared = {}
    for n, a in inputs.items():
        if n == "L0_attn_x" or n not in want:
            continue
        shared[n] = np.ascontiguousarray(np.asarray(a, dtype=np.float32))
    in_maps = []
    for c in range(ncores):
        bb, hh = c // 2, c % 2
        m = dict(shared)
        if "L0_attn_x" in want:
            m["L0_attn_x"] = np.ascontiguousarray(x[bb])
        if "qoff" in want:
            qoff = hh * (T // 2)
            m["qoff"] = np.array([qoff, (T // 2) - qoff], dtype=np.int32)
        if "aux_off" in want:
            TOWN = T // 2
            EXT_W, ESH, NH1 = 8576, 4224, 16
            NRT = TOWN // 128
            WLO = 128 * (NRT - 1)
            qoff = hh * TOWN
            a = np.zeros(64, dtype=np.int32)
            a[0] = qoff
            for h in range(NH1):
                a[1 + h] = h * EXT_W + ESH - WLO - qoff
                a[17 + h] = h * EXT_W + ESH - (T - 128) + qoff
            m["aux_off"] = a
        if "gidx" in want:
            TOWN = T // 2
            EXT_W, ESH, NH1 = 8576, 4224, 16
            PADW = EXT_W - 127
            NRT = TOWN // 128
            WLO = 128 * (NRT - 1)
            qoff = hh * TOWN
            gi = np.zeros((128, 64), dtype=np.int32)
            p = np.arange(128)
            off1 = ESH - WLO - 127 - qoff
            off2 = ESH - (T - 128) - 127 + qoff
            for h2 in range(NH1):
                gi[:, h2] = h2 * 128 * PADW + p * PADW + off1
                gi[:, 16 + h2] = h2 * 128 * PADW + p * PADW + off2
            for c2 in range(8):
                gi[:, 40 + c2] = p * 8 * T + c2 * T + qoff
            for i2 in range(NRT):
                gi[:, 24 + i2] = (qoff + i2 * 128 + p) * 1024
            m["gidx"] = gi
        in_maps.append(m)
    return in_maps


def kernel(**inputs):
    cfg = dict(T=4096, TOWN=2048, stage="final")
    nc = _get_prog(("final", 4096), cfg)
    in_maps = make_in_maps(inputs, nc)
    res = run_bass_kernel_spmd(nc, in_maps, core_ids=list(range(8)))
    out = np.zeros((4, 4096, 1024), dtype=np.float32)
    for c in range(8):
        bb, hh = c // 2, c % 2
        out[bb, hh * 2048:(hh + 1) * 2048] = res.results[c]["y"]
    return out
